# version 04
# baseline (speedup 1.0000x reference)
from contextlib import ExitStack

import numpy as np
import ml_dtypes
import concourse.bass as bass
import concourse.mybir as mybir
from concourse.bass_utils import run_bass_kernel_spmd

F32 = mybir.dt.float32
BF16 = mybir.dt.bfloat16
AOP = mybir.AluOpType
AF = mybir.ActivationFunctionType
BFNP = ml_dtypes.bfloat16

B, L, D = 2, 16384, 1024
T = 4096
DI = 2048
NST = 16
DTR = 64
DSL = 512
DFF = 4096
H, HD = 16, 64
EPS_LN = 1e-5
EPS_AT = 1e-6
CH = 512
NCH = T // CH

ENGINES = ("sync", "gpsimd", "vector", "scalar", "tensor")


class Sched:

    SELF_SEM = {"vector": "v", "scalar": "s"}

    def __init__(self, nc):
        self.nc = nc
        self.ops = []
        self.cnt = {}

    def n(self, sem):
        return self.cnt.get(sem, 0)

    def op(self, engine, emit, waits=(), inc=None, amt=1):
        ss = self.SELF_SEM.get(engine)
        selfwait = 0
        if ss is not None:
            if inc is None:
                inc, amt = ss, 1
            assert inc == ss, f"{engine} op must inc {ss}"
            selfwait = self.cnt.get(ss, 0)
        if inc is not None:
            self.cnt[inc] = self.cnt.get(inc, 0) + amt
        self.ops.append((engine, emit, tuple(waits), inc, amt, selfwait))
        return self.cnt.get(inc, 0) if inc else None

    def bar(self):
        return tuple(self.cnt.items())

    def emit(self, stack):
        sems = {s: stack.enter_context(self.nc.semaphore(s)) for s in self.cnt}
        blk = stack.enter_context(self.nc.Block())
        for eng in ENGINES:
            ops = [o for o in self.ops if o[0] == eng]
            if not ops:
                continue

            def body(h, ops=ops, eng=eng):
                seen = {}
                ss = Sched.SELF_SEM.get(eng)
                for (_, emit, waits, inc, amt, selfwait) in ops:
                    wl = list(waits)
                    if ss is not None and selfwait > 0:
                        wl.append((ss, selfwait))
                    for (sname, v) in wl:
                        if v > 0 and seen.get(sname, 0) < v and sname in sems:
                            h.wait_ge(sems[sname], v)
                            seen[sname] = v
                    instr = emit(h)
                    if inc is not None:
                        instr.then_inc(sems[inc], amt)

            getattr(blk, eng)(body)


def sbuf(stack, nc, name, shape, dt):
    return stack.enter_context(nc.sbuf_tensor(name, list(shape), dt))


def psum_t(stack, nc, name, shape, dt=F32):
    return stack.enter_context(nc.psum_tensor(name, list(shape), dt))


def load_weight_bf16(sc, st, nc, name, hbm_ap, k_rows, cols, stage):
    nk = (k_rows + 127) // 128
    sw = stage.shape[1]
    tiles = []
    for k in range(nk):
        r0 = k * 128
        rr = min(128, k_rows - r0)
        tile = sbuf(st, nc, f"{name}_bf{k}", (rr, cols), BF16)
        for c0 in range(0, cols, sw):
            cw = min(sw, cols - c0)
            d = sc.op("sync", lambda h, r0=r0, rr=rr, c0=c0, cw=cw:
                      h.dma_start(out=stage[0:rr, 0:cw],
                                  in_=hbm_ap[r0:r0 + rr, c0:c0 + cw]),
                      waits=(("v", sc.n("v")),), inc="yd", amt=16)
            sc.op("vector", lambda h, t=tile, rr=rr, c0=c0, cw=cw:
                  h.tensor_copy(t[0:rr, c0:c0 + cw], stage[0:rr, 0:cw]),
                  waits=(("yd", d),))
        tiles.append(tile)
    return tiles


class LnKit:

    def __init__(self, st, nc, tag, w):
        self.w = w
        self.ps_sum = psum_t(st, nc, f"{tag}_pssum", (128, 512))
        self.ps_sq = psum_t(st, nc, f"{tag}_pssq", (128, 512))
        self.ps_rep = psum_t(st, nc, f"{tag}_psrep", (128, 512))
        self.ps_fl = psum_t(st, nc, f"{tag}_psfl", (128, 512))
        self.rstd_b = sbuf(st, nc, f"{tag}_rstdb", (1, w), BF16)
        self.nsc_b = sbuf(st, nc, f"{tag}_nscb", (1, w), BF16)
        self.tmp = [sbuf(st, nc, f"{tag}_tmp{i}", (128, w), F32) for i in range(2)]
        self.ssum = sbuf(st, nc, f"{tag}_ssum", (1, w), F32)
        self.ssq = sbuf(st, nc, f"{tag}_ssq", (1, w), F32)
        self.sc1 = sbuf(st, nc, f"{tag}_sc1", (1, w), F32)
        self.mean = sbuf(st, nc, f"{tag}_mean", (1, w), F32)
        self.var = sbuf(st, nc, f"{tag}_var", (1, w), F32)
        self.rstd = sbuf(st, nc, f"{tag}_rstd", (1, w), F32)
        self.nsc = sbuf(st, nc, f"{tag}_nsc", (1, w), F32)
        self.rrep = sbuf(st, nc, f"{tag}_rrep", (128, w), F32)
        self.nrep = sbuf(st, nc, f"{tag}_nrep", (128, w), F32)


def emit_ln(sc, kit, x_tiles, nblk, w, xln_tiles, ones_col, ones_row,
            waits0=()):
    k9 = kit
    nd = nblk * 128
    tmm = []
    for k in range(nblk):
        tmm.append(sc.op("tensor", lambda h, k=k: h.matmul(
            k9.ps_sum[0:1, 0:w], ones_col[:], x_tiles[k][:, 0:w],
            start=(k == 0), stop=(k == nblk - 1)),
            waits=waits0 if k == 0 else (), inc="t"))
    tsum = tmm[-1]
    sqmm = []
    for k in range(nblk):
        wl = list(waits0) if k == 0 else []
        if k >= 2:
            wl.append(("t", sqmm[k - 2]))
        sq = sc.op("vector", lambda h, k=k: h.tensor_mul(
            k9.tmp[k % 2][0:128, 0:w], x_tiles[k][:, 0:w],
            x_tiles[k][:, 0:w]), waits=tuple(wl), inc="v")
        sqmm.append(sc.op("tensor", lambda h, k=k: h.matmul(
            k9.ps_sq[0:1, 0:w], ones_col[:], k9.tmp[k % 2][0:128, 0:w],
            start=(k == 0), stop=(k == nblk - 1)),
            waits=(("v", sq),), inc="t"))
    fl = sc.op("tensor", lambda h: h.matmul(
        k9.ps_fl[0:1, 0:1], ones_col[:], ones_col[:],
        start=True, stop=True), inc="t")
    sc.op("vector", lambda h: h.tensor_copy(
        k9.ssum[0:1, 0:w], k9.ps_sum[0:1, 0:w]), waits=(("t", fl),), inc="v")
    sc.op("vector", lambda h: h.tensor_copy(
        k9.ssq[0:1, 0:w], k9.ps_sq[0:1, 0:w]), inc="v")
    sc.op("vector", lambda h: h.tensor_scalar_mul(
        k9.mean[0:1, 0:w], k9.ssum[0:1, 0:w], 1.0 / nd), inc="v")
    sc.op("vector", lambda h: h.tensor_mul(
        k9.nsc[0:1, 0:w], k9.mean[0:1, 0:w], k9.mean[0:1, 0:w]), inc="v")
    sc.op("vector", lambda h: h.tensor_scalar_mul(
        k9.sc1[0:1, 0:w], k9.ssq[0:1, 0:w], 1.0 / nd), inc="v")
    sc.op("vector", lambda h: h.tensor_sub(
        k9.var[0:1, 0:w], k9.sc1[0:1, 0:w], k9.nsc[0:1, 0:w]), inc="v")
    v3 = sc.op("vector", lambda h: h.tensor_scalar_add(
        k9.var[0:1, 0:w], k9.var[0:1, 0:w], EPS_LN), inc="v")
    s1 = sc.op("scalar", lambda h: h.activation(
        k9.rstd[0:1, 0:w], k9.var[0:1, 0:w], AF.Sqrt),
        waits=(("v", v3),), inc="s")
    v4 = sc.op("vector", lambda h: h.reciprocal(
        k9.rstd[0:1, 0:w], k9.rstd[0:1, 0:w]), waits=(("s", s1),), inc="v")
    sc.op("vector", lambda h: h.tensor_mul(
        k9.sc1[0:1, 0:w], k9.mean[0:1, 0:w], k9.rstd[0:1, 0:w]),
        waits=(("v", v4),), inc="v")
    sc.op("vector", lambda h: h.tensor_scalar_mul(
        k9.nsc[0:1, 0:w], k9.sc1[0:1, 0:w], -1.0), inc="v")
    sc.op("vector", lambda h: h.tensor_copy(
        k9.rstd_b[0:1, 0:w], k9.rstd[0:1, 0:w]), inc="v")
    v5 = sc.op("vector", lambda h: h.tensor_copy(
        k9.nsc_b[0:1, 0:w], k9.nsc[0:1, 0:w]), inc="v")
    t1 = sc.op("tensor", lambda h: h.matmul(
        k9.ps_rep[0:128, 0:w], ones_row[:], k9.rstd_b[0:1, 0:w],
        start=True, stop=True), waits=(("v", v5),), inc="t")
    v6 = sc.op("vector", lambda h: h.tensor_copy(
        k9.rrep[0:128, 0:w], k9.ps_rep[0:128, 0:w]),
        waits=(("t", t1),), inc="v")
    t2 = sc.op("tensor", lambda h: h.matmul(
        k9.ps_rep[0:128, 0:w], ones_row[:], k9.nsc_b[0:1, 0:w],
        start=True, stop=True), waits=(("v", v6),), inc="t")
    v7 = sc.op("vector", lambda h: h.tensor_copy(
        k9.nrep[0:128, 0:w], k9.ps_rep[0:128, 0:w]),
        waits=(("t", t2),), inc="v")
    last = v7
    for k in range(nblk):
        sc.op("vector", lambda h, k=k: h.tensor_mul(
            k9.tmp[k % 2][0:128, 0:w], x_tiles[k][:, 0:w], k9.rrep[0:128, 0:w]),
            waits=(("v", v7),), inc="v")
        last = sc.op("vector", lambda h, k=k: h.tensor_add(
            xln_tiles[k][:, 0:w], k9.tmp[k % 2][0:128, 0:w],
            k9.nrep[0:128, 0:w]), inc="v")
    return last


def build_L1():
    nc = bass.Bass("TRN2", target_bir_lowering=False)
    xT = nc.dram_tensor("xT", [D, T + 3], F32, kind="ExternalInput").ap()
    w_in = nc.dram_tensor("w_in", [D, 2 * DI], F32, kind="ExternalInput").ap()
    conv_w = nc.dram_tensor("conv_w", [128, 64], F32, kind="ExternalInput").ap()
    conv_b = nc.dram_tensor("conv_b", [128, 16], F32, kind="ExternalInput").ap()
    w_x = nc.dram_tensor("w_x", [DI, 128], F32, kind="ExternalInput").ap()
    w_dt = nc.dram_tensor("w_dt", [DTR, DI], F32, kind="ExternalInput").ap()
    b_dt = nc.dram_tensor("b_dt", [128, 16], F32, kind="ExternalInput").ap()

    o_du = nc.dram_tensor("o_du", [4, 2, DSL, T], BF16, kind="ExternalOutput").ap()
    o_bc = nc.dram_tensor("o_bc", [32, T], BF16, kind="ExternalOutput").ap()
    o_res = nc.dram_tensor("o_res", [DI, T], BF16, kind="ExternalOutput").ap()

    with ExitStack() as st:
        sc = Sched(nc)
        stage = sbuf(st, nc, "stage", (128, 4096), F32)
        win_t = load_weight_bf16(sc, st, nc, "win", w_in, D, 2 * DI, stage)
        wx_t = load_weight_bf16(sc, st, nc, "wx", w_x, DI, 128, stage)
        wdt_t = load_weight_bf16(sc, st, nc, "wdt", w_dt, DTR, DI, stage)
        cw_sb = sbuf(st, nc, "cw", (128, 64), F32)
        cb_sb = sbuf(st, nc, "cb", (128, 16), F32)
        bdt_sb = sbuf(st, nc, "bdt", (128, 16), F32)
        ones_col = sbuf(st, nc, "ones_col", (128, 1), F32)
        neg_col = sbuf(st, nc, "neg_col", (128, 1), F32)
        ones_row = sbuf(st, nc, "ones_row", (1, 128), BF16)
        sc.op("sync", lambda h: h.dma_start(out=cw_sb[:], in_=conv_w),
              inc="yd", amt=16)
        sc.op("sync", lambda h: h.dma_start(out=cb_sb[:], in_=conv_b),
              inc="yd", amt=16)
        sc.op("sync", lambda h: h.dma_start(out=bdt_sb[:], in_=b_dt),
              inc="yd", amt=16)
        sc.op("vector", lambda h: h.memset(ones_col[:], 1.0), inc="v")
        sc.op("vector", lambda h: h.memset(neg_col[:], -1.0), inc="v")
        sc.op("vector", lambda h: h.memset(ones_row[:], 1.0), inc="v")

        x_sb = [sbuf(st, nc, f"x{k}", (128, CH), F32) for k in range(8)]
        ln = LnKit(st, nc, "ln", CH)
        xln = [sbuf(st, nc, f"xln{k}", (128, CH), BF16) for k in range(8)]
        ps_mm = [psum_t(st, nc, f"ps_mm{i}", (128, CH)) for i in range(2)]
        x_in = [sbuf(st, nc, f"xin{k}", (128, CH + 3), F32) for k in range(16)]
        conv_o = [sbuf(st, nc, f"conv_o{i}", (128, CH), F32) for i in range(2)]
        u_t = [sbuf(st, nc, f"u{k}", (128, CH), BF16) for k in range(16)]
        res_b = [sbuf(st, nc, f"res_b{i}", (128, CH), BF16) for i in range(2)]
        dt_t = sbuf(st, nc, "dt_t", (64, CH), BF16)
        b_t = sbuf(st, nc, "b_t", (16, CH), BF16)
        c_t = sbuf(st, nc, "c_t", (16, CH), BF16)
        sp_a = sbuf(st, nc, "sp_a", (128, CH), F32)
        sp_e = sbuf(st, nc, "sp_e", (128, CH), F32)
        sp_l = sbuf(st, nc, "sp_l", (128, CH), F32)
        sp_r = sbuf(st, nc, "sp_r", (128, CH), F32)
        dl_t = [sbuf(st, nc, f"dl_t{i}", (128, CH), BF16) for i in range(2)]

        wbar = sc.bar()

        hw = []
        for k in range(8):
            hw.append(sc.op("sync", lambda h, k=k: h.dma_start(
                out=x_sb[k][:, 0:3], in_=xT[k * 128:(k + 1) * 128, 0:3]),
                waits=wbar, inc="yd", amt=16))
        lnh = emit_ln(sc, ln, x_sb, 8, 3, xln, ones_col, ones_row,
                      waits0=(("yd", hw[-1]), ("v", sc.n("v"))))
        for m in range(16):
            pm = ps_mm[m % 2]
            for k in range(8):
                sc.op("tensor", lambda h, m=m, k=k, pm=pm: h.matmul(
                    pm[0:128, 0:3], win_t[k][:, m * 128:(m + 1) * 128],
                    xln[k][:, 0:3], start=(k == 0), stop=(k == 7)),
                    waits=(("v", sc.n("v")),) if k == 0 else (),
                    inc=("t" if k == 7 else None))
            sc.op("vector", lambda h, m=m, pm=pm: h.tensor_copy(
                x_in[m][:, 0:3], pm[0:128, 0:3]),
                waits=(("t", sc.n("t")),), inc="v")

        for ci in range(NCH):
            c0 = 3 + ci * CH
            xw = []
            for k in range(8):
                xw.append(sc.op("sync", lambda h, k=k, c0=c0: h.dma_start(
                    out=x_sb[k][:], in_=xT[k * 128:(k + 1) * 128, c0:c0 + CH]),
                    waits=(("v", sc.n("v")), ("s", sc.n("s")),
                           ("t", sc.n("t"))),
                    inc="yd", amt=16))
            lnd = emit_ln(sc, ln, x_sb, 8, CH, xln, ones_col, ones_row,
                          waits0=(("yd", xw[-1]), ("v", sc.n("v"))))
            if ci > 0:
                for m in range(16):
                    sc.op("vector", lambda h, m=m: h.tensor_copy(
                        x_in[m][:, 0:3], x_in[m][:, CH:CH + 3]), inc="v")
            carry_done = sc.n("v")
            evac = {}
            res_gd = getattr(sc, "_res_gd", {})
            sc._res_gd = res_gd
            for m in range(32):
                pm = ps_mm[m % 2]
                wl = [("v", lnd), ("v", carry_done)]
                if m >= 2:
                    wl.append(("v", evac[m - 2]))
                for k in range(8):
                    sc.op("tensor", lambda h, m=m, k=k, pm=pm: h.matmul(
                        pm[:], win_t[k][:, m * 128:(m + 1) * 128], xln[k][:],
                        start=(k == 0), stop=(k == 7)),
                        waits=tuple(wl) if k == 0 else (),
                        inc=("t" if k == 7 else None))
                tm = sc.n("t")
                if m < 16:
                    evac[m] = sc.op("vector", lambda h, m=m, pm=pm:
                                    h.tensor_copy(x_in[m][:, 3:CH + 3], pm[:]),
                                    waits=(("t", tm),), inc="v")
                else:
                    pr = m % 2
                    vr = sc.op("vector", lambda h, pm=pm, pr=pr: h.tensor_copy(
                        res_b[pr][:], pm[:]),
                        waits=(("t", tm), ("gd", res_gd.get(pr, 0))), inc="v")
                    res_gd[pr] = sc.op("gpsimd",
                        lambda h, m=m, ci=ci, pr=pr: h.dma_start(
                            out=o_res[(m - 16) * 128:(m - 15) * 128,
                                      ci * CH:(ci + 1) * CH],
                            in_=res_b[pr][:]), waits=(("v", vr),),
                        inc="gd", amt=16)
                    evac[m] = vr
            u_gd = getattr(sc, "_u_gd", {})
            sc._u_gd = u_gd
            for k in range(16):
                sc.op("vector", lambda h, k=k: h.tensor_scalar(
                    conv_o[0][:], x_in[k][:, 0:CH], cw_sb[:, 4 * k:4 * k + 1],
                    cb_sb[:, k:k + 1], AOP.mult, AOP.add),
                    waits=(("v", evac[k]), ("s", sc.n("s")),
                           ("t", sc.n("t"))), inc="v")
                for tap in range(1, 4):
                    vk = sc.op("vector", lambda h, k=k, tap=tap:
                               h.scalar_tensor_tensor(
                                   conv_o[tap % 2][:],
                                   x_in[k][:, tap:tap + CH],
                                   cw_sb[:, 4 * k + tap:4 * k + tap + 1],
                                   conv_o[(tap + 1) % 2][:],
                                   AOP.mult, AOP.add), inc="v")
                sk = sc.op("scalar", lambda h, k=k: h.activation(
                    u_t[k][:], conv_o[1][:], AF.Silu),
                    waits=(("v", vk), ("gd", u_gd.get(k, 0))), inc="s")
                u_gd[k] = sc.op("gpsimd", lambda h, k=k, ci=ci: h.dma_start(
                    out=o_du[k // 4, 1, (k % 4) * 128:(k % 4 + 1) * 128,
                             ci * CH:(ci + 1) * CH],
                    in_=u_t[k][:]), waits=(("s", sk),), inc="gd", amt=16)
            ubar = sc.n("s")
            for k in range(16):
                sc.op("tensor", lambda h, k=k: h.matmul(
                    ln.ps_rep[:], wx_t[k][:], u_t[k][:],
                    start=(k == 0), stop=(k == 15)),
                    waits=(("s", ubar), ("v", sc.n("v"))) if k == 0 else (),
                    inc=("t" if k == 15 else None))
            tx = sc.n("t")
            gprev = sc.n("gd")
            vdt = sc.op("vector", lambda h: h.tensor_copy(
                dt_t[:], ln.ps_rep[0:64, :]),
                waits=(("t", tx), ("gd", gprev)), inc="v")
            vb = sc.op("vector", lambda h: h.tensor_copy(
                b_t[:], ln.ps_rep[64:80, :]), inc="v")
            vc = sc.op("vector", lambda h: h.tensor_copy(
                c_t[:], ln.ps_rep[96:112, :]), inc="v")
            sc.op("gpsimd", lambda h, ci=ci: h.dma_start(
                out=o_bc[0:16, ci * CH:(ci + 1) * CH], in_=b_t[:]),
                waits=(("v", vb),), inc="gd", amt=16)
            sc.op("gpsimd", lambda h, ci=ci: h.dma_start(
                out=o_bc[16:32, ci * CH:(ci + 1) * CH], in_=c_t[:]),
                waits=(("v", vc),), inc="gd", amt=16)
            dprev = {}
            dl_gd = getattr(sc, "_dl_gd", {})
            sc._dl_gd = dl_gd
            for m in range(16):
                pd = ps_mm[m % 2]
                wl = [("v", vdt)]
                if m >= 2:
                    wl.append(("s", dprev[m - 2]))
                else:
                    wl.append(("v", evac[31]))
                sc.op("tensor", lambda h, m=m, pd=pd: h.matmul(
                    pd[:], wdt_t[0][:, m * 128:(m + 1) * 128], dt_t[:],
                    start=True, stop=True), waits=tuple(wl), inc="t")
                td = sc.n("t")
                sc.op("scalar", lambda h, m=m, pd=pd: h.activation(
                    sp_a[:], pd[:], AF.Abs, bias=bdt_sb[:, m:m + 1]),
                    waits=(("t", td), ("v", sc.n("v"))), inc="s")
                sc.op("scalar", lambda h: h.activation(
                    sp_e[:], sp_a[:], AF.Exp, scale=neg_col[:, 0:1]), inc="s")
                sc.op("scalar", lambda h: h.activation(
                    sp_l[:], sp_e[:], AF.Ln, bias=ones_col[:, 0:1]), inc="s")
                dprev[m] = sc.op("scalar", lambda h, m=m, pd=pd: h.activation(
                    sp_r[:], pd[:], AF.Relu, bias=bdt_sb[:, m:m + 1]), inc="s")
                pr = m % 2
                vd = sc.op("vector", lambda h, pr=pr: h.tensor_add(
                    dl_t[pr][:], sp_l[:], sp_r[:]),
                    waits=(("s", dprev[m]), ("gd", dl_gd.get(pr, 0))),
                    inc="v")
                dl_gd[pr] = sc.op("gpsimd", lambda h, m=m, ci=ci, pr=pr:
                                  h.dma_start(
                    out=o_du[m // 4, 0, (m % 4) * 128:(m % 4 + 1) * 128,
                             ci * CH:(ci + 1) * CH],
                    in_=dl_t[pr][:]), waits=(("v", vd),), inc="gd", amt=16)
        sc.op("gpsimd", lambda h: h.engine_nop(),
              waits=(("gd", sc.n("gd")), ("yd", sc.n("yd"))), inc=None)
        sc.emit(st)
    return nc


def _prep_L1_inputs(inputs, core):
    x = np.asarray(inputs["L0_attn_a_x"], dtype=np.float32).reshape(B * L, D)
    t0 = core * T
    xh = np.zeros((T + 3, D), dtype=np.float32)
    if core % 4 == 0:
        xh[3:] = x[t0:t0 + T]
    else:
        xh[:] = x[t0 - 3:t0 + T]
    w_x = np.asarray(inputs["L0_attn_a_W_x"], dtype=np.float32)
    w_x_pad = np.zeros((DI, 128), dtype=np.float32)
    w_x_pad[:, 0:64] = w_x[:, 0:DTR]
    w_x_pad[:, 64:80] = w_x[:, DTR:DTR + NST]
    w_x_pad[:, 96:112] = w_x[:, DTR + NST:]
    return {
        "xT": np.ascontiguousarray(xh.T),
        "w_in": np.asarray(inputs["L0_attn_a_W_in"], dtype=np.float32),
        "conv_w": np.ascontiguousarray(
            np.asarray(inputs["L0_attn_a_conv_w"], dtype=np.float32)
            .reshape(16, 128, 4).transpose(1, 0, 2).reshape(128, 64)),
        "conv_b": np.ascontiguousarray(
            np.asarray(inputs["L0_attn_a_conv_b"], dtype=np.float32)
            .reshape(16, 128).T),
        "w_x": w_x_pad,
        "w_dt": np.asarray(inputs["L0_attn_a_W_dt"], dtype=np.float32),
        "b_dt": np.ascontiguousarray(
            np.asarray(inputs["L0_attn_a_b_dt"], dtype=np.float32)
            .reshape(16, 128).T),
    }




TQ = 2048
NTQ = T // TQ


def build_L2():
    nc = bass.Bass("TRN2", target_bir_lowering=False)
    du_in = nc.dram_tensor("du_in", [4, 2, DSL, T], BF16, kind="ExternalInput").ap()
    bc_in = nc.dram_tensor("bc_in", [4, 32, T], BF16, kind="ExternalInput").ap()
    a_log = nc.dram_tensor("a_log", [DSL, NST], F32, kind="ExternalInput").ap()
    repsel = nc.dram_tensor("repsel", [NST, NST * 128], F32, kind="ExternalInput").ap()
    ident = nc.dram_tensor("ident", [128, 128], F32, kind="ExternalInput").ap()
    o_y = nc.dram_tensor("o_y", [4, DSL, T], BF16, kind="ExternalOutput").ap()

    NI = DSL // 128

    with ExitStack() as st:
        sc = Sched(nc)
        stage = sbuf(st, nc, "stage", (128, 512), F32)
        sel_bf = sbuf(st, nc, "sel_bf", (NST, NST * 128), BF16)
        a_sb = [sbuf(st, nc, f"a{i}", (128, NST), F32) for i in range(NI)]
        for cc in range(4):
            d0 = sc.op("sync", lambda h, cc=cc: h.dma_start(
                out=stage[0:NST, 0:512], in_=repsel[:, cc * 512:(cc + 1) * 512]),
                waits=(("v", sc.n("v")),), inc="yd", amt=16)
            sc.op("vector", lambda h, cc=cc: h.tensor_copy(
                sel_bf[:, cc * 512:(cc + 1) * 512], stage[0:NST, 0:512]),
                waits=(("yd", d0),))
        id_bf = sbuf(st, nc, "id_bf", (128, 128), BF16)
        d0 = sc.op("sync", lambda h: h.dma_start(
            out=stage[0:128, 0:128], in_=ident),
            waits=(("v", sc.n("v")),), inc="yd", amt=16)
        sc.op("vector", lambda h: h.tensor_copy(
            id_bf[:], stage[0:128, 0:128]), waits=(("yd", d0),))
        d0 = sc.op("sync", lambda h: h.dma_start(
            out=stage[0:128, 0:NST * NI].rearrange("p (i n) -> p i n", i=NI),
            in_=a_log.rearrange("(i p) n -> p i n", p=128)),
            waits=(("v", sc.n("v")),), inc="yd", amt=16)
        for i in range(NI):
            sq = sc.op("scalar", lambda h, i=i: h.activation(
                a_sb[i][:], stage[0:128, NST * i:NST * (i + 1)], AF.Exp),
                waits=(("yd", d0),))
            sc.op("vector", lambda h, i=i: h.tensor_scalar_mul(
                a_sb[i][:], a_sb[i][:], -1.0), waits=(("s", sq),))

        hcarry = sbuf(st, nc, "hcarry", (128, NI * NST), F32)
        sc.op("vector", lambda h: h.memset(hcarry[:], 0.0))

        bsb = sbuf(st, nc, "bsb", (NST, TQ), BF16)
        csb = sbuf(st, nc, "csb", (NST, TQ), BF16)
        brep = [sbuf(st, nc, f"brep{n}", (128, TQ), BF16) for n in range(NST)]
        crep = [sbuf(st, nc, f"crep{n}", (128, TQ), BF16) for n in range(NST)]
        dl_t = [sbuf(st, nc, f"dl{i}", (128, TQ), BF16) for i in range(2)]
        u_tl = [sbuf(st, nc, f"uu{i}", (128, TQ), BF16) for i in range(2)]
        du_sb = sbuf(st, nc, "du_sb", (128, TQ), BF16)
        dA = [sbuf(st, nc, f"dA{i}", (128, TQ), BF16) for i in range(2)]
        dBu = sbuf(st, nc, "dBu", (128, TQ), BF16)
        h_t = sbuf(st, nc, "h_t", (128, TQ), BF16)
        hc_t = [sbuf(st, nc, f"hc{i}", (128, TQ), BF16) for i in range(2)]
        y_ev = [sbuf(st, nc, f"y_ev{i}", (128, TQ), BF16) for i in range(2)]
        ps_rep = psum_t(st, nc, "ps_rep", (128, 512))
        ps_y = psum_t(st, nc, "ps_y", (128, TQ))

        wbar = sc.bar()
        yev_gd = {0: 0, 1: 0}
        last_vev = 0
        prev_mm = 0
        for j in range(4):
            for tq in range(NTQ):
                t0 = tq * TQ
                dbc = sc.op("sync", lambda h, j=j, t0=t0: h.dma_start(
                    out=bsb[:], in_=bc_in[j, 0:NST, t0:t0 + TQ]),
                    waits=wbar + (("t", sc.n("t")), ("v", sc.n("v"))),
                    inc="yd", amt=16)
                dbc = sc.op("sync", lambda h, j=j, t0=t0: h.dma_start(
                    out=csb[:], in_=bc_in[j, NST:2 * NST, t0:t0 + TQ]),
                    inc="yd", amt=16)
                for n in range(NST):
                    for half, (src, dstl) in enumerate([(bsb, brep),
                                                        (csb, crep)]):
                        for ch in range(TQ // 512):
                            tm = sc.op("tensor", lambda h, n=n, src=src, ch=ch,
                                       t0=t0: h.matmul(
                                ps_rep[:], sel_bf[:, n * 128:(n + 1) * 128],
                                src[:, ch * 512:(ch + 1) * 512],
                                start=True, stop=True),
                                waits=(("yd", dbc), ("v", sc.n("v"))), inc="t")
                            sc.op("vector", lambda h, n=n, dstl=dstl, ch=ch:
                                  h.tensor_copy(
                                      dstl[n][:, ch * 512:(ch + 1) * 512],
                                      ps_rep[:]),
                                  waits=(("t", tm),))
                repbar = sc.n("v")
                for i in range(NI):
                    par_i = i % 2
                    dld = sc.op("sync", lambda h, j=j, i=i, t0=t0, par_i=par_i:
                                h.dma_start(out=dl_t[par_i][:],
                                            in_=du_in[j, 0,
                                                      i * 128:(i + 1) * 128,
                                                      t0:t0 + TQ]),
                                waits=(("s", sc.n("s")), ("v", sc.n("v"))),
                                inc="yd", amt=16)
                    dld = sc.op("sync", lambda h, j=j, i=i, t0=t0, par_i=par_i:
                                h.dma_start(out=u_tl[par_i][:],
                                            in_=du_in[j, 1,
                                                      i * 128:(i + 1) * 128,
                                                      t0:t0 + TQ]),
                                inc="yd", amt=16)
                    vdu = sc.op("vector", lambda h, par_i=par_i: h.tensor_mul(
                        du_sb[:], dl_t[par_i][:], u_tl[par_i][:]),
                        waits=(("yd", dld), ("v", repbar)))
                    scan_done = {}
                    mm_done = {-2: prev_mm, -1: prev_mm}
                    for n in range(NST):
                        par = n % 2
                        col = i * NST + n
                        sdA = sc.op("scalar", lambda h, i=i, n=n, par=par,
                                    par_i=par_i: h.activation(
                                        dA[par][:], dl_t[par_i][:],
                                        AF.Exp, scale=a_sb[i][:, n:n + 1]),
                                    waits=(("v", scan_done.get(n - 2, vdu)),))
                        sc.op("vector", lambda h, n=n: h.tensor_mul(
                            dBu[:], du_sb[:], brep[n][:]))
                        scan_done[n] = sc.op("vector",
                            lambda h, par=par, col=col: h.tensor_tensor_scan(
                                h_t[:], dA[par][:], dBu[:],
                                hcarry[:, col:col + 1], AOP.mult, AOP.add),
                            waits=(("s", sdA),))
                        sc.op("vector", lambda h, col=col: h.tensor_copy(
                            hcarry[:, col:col + 1], h_t[:, TQ - 1:TQ]))
                        vhc = sc.op("vector", lambda h, n=n, par=par:
                                    h.tensor_mul(hc_t[par][:], h_t[:],
                                                 crep[n][:]),
                                    waits=(("t", mm_done[n - 2]),))
                        for ch in range(TQ // 512):
                            mm_done[n] = sc.op("tensor",
                                lambda h, par=par, ch=ch, n=n: h.matmul(
                                    ps_y[:, ch * 512:(ch + 1) * 512],
                                    id_bf[:],
                                    hc_t[par][:, ch * 512:(ch + 1) * 512],
                                    start=(n == 0), stop=(n == NST - 1)),
                                waits=((("v", vhc),) if ch == 0 else ()) +
                                      ((("v", last_vev),)
                                       if n == 0 and ch == 0 else ()),
                                inc="t")
                    prev_mm = mm_done[NST - 1]
                    par_y = (j * NTQ * NI + tq * NI + i) % 2
                    last_vev = sc.op("vector",
                        lambda h, par_y=par_y: h.tensor_copy(
                            y_ev[par_y][:], ps_y[:]),
                        waits=(("t", prev_mm), ("gd", yev_gd[par_y])))
                    yev_gd[par_y] = sc.op("gpsimd",
                        lambda h, j=j, i=i, t0=t0, par_y=par_y: h.dma_start(
                            out=o_y[j, i * 128:(i + 1) * 128, t0:t0 + TQ],
                            in_=y_ev[par_y][:]), waits=(("v", last_vev),),
                        inc="gd", amt=16)
        sc.op("gpsimd", lambda h: h.engine_nop(),
              waits=(("gd", sc.n("gd")), ("yd", sc.n("yd"))), inc=None)
        sc.emit(st)
    return nc


def _prep_L2_inputs(l1_results, inputs, core):
    g = (core // 4) * 4
    cr = core % 4
    du = np.stack([np.asarray(l1_results[g + j]["o_du"])[cr]
                   for j in range(4)])
    bc = np.stack([np.asarray(l1_results[g + j]["o_bc"]) for j in range(4)])
    a_log = np.asarray(inputs["L0_attn_a_A_log"],
                       dtype=np.float32)[cr * DSL:(cr + 1) * DSL]
    repsel = np.zeros((NST, NST * 128), dtype=np.float32)
    for n in range(NST):
        repsel[n, n * 128:(n + 1) * 128] = 1.0
    return {"du_in": du, "bc_in": bc, "a_log": np.ascontiguousarray(a_log),
            "repsel": repsel, "ident": np.eye(128, dtype=np.float32)}


GELU_A = 1.5957691216057308
GELU_B = GELU_A * 0.044715


def emit_matmul_evac(sc, ps, lhsT_tiles, mcol, rhs_tiles, w, nk, waits0=()):
    tm = None
    for k in range(nk):
        tm = sc.op("tensor", lambda h, k=k: h.matmul(
            ps[0:128, 0:w], lhsT_tiles[k][:, mcol:mcol + 128],
            rhs_tiles[k][:, 0:w], start=(k == 0), stop=(k == nk - 1)),
            waits=waits0 if k == 0 else (), inc="t")
    return tm


SQ = 256
NSQ = T // SQ


def build_L3():
    nc = bass.Bass("TRN2", target_bir_lowering=False)
    xT = nc.dram_tensor("xT", [D, T], F32, kind="ExternalInput").ap()
    y_in = nc.dram_tensor("y_in", [DI, T], BF16, kind="ExternalInput").ap()
    res_in = nc.dram_tensor("res_in", [DI, T], BF16, kind="ExternalInput").ap()
    u_in = nc.dram_tensor("u_in", [DI, T], BF16, kind="ExternalInput").ap()
    d_vec = nc.dram_tensor("d_vec", [128, 16], F32, kind="ExternalInput").ap()
    w_out = nc.dram_tensor("w_out", [DI, D], F32, kind="ExternalInput").ap()
    w1 = nc.dram_tensor("w1", [D, DFF], F32, kind="ExternalInput").ap()
    b1 = nc.dram_tensor("b1", [128, 32], F32, kind="ExternalInput").ap()
    w2 = nc.dram_tensor("w2", [DFF, D], F32, kind="ExternalInput").ap()
    b2 = nc.dram_tensor("b2", [128, 8], F32, kind="ExternalInput").ap()
    wqT = nc.dram_tensor("wqT", [D, D], F32, kind="ExternalInput").ap()
    wkT = nc.dram_tensor("wkT", [D, D], F32, kind="ExternalInput").ap()
    wvT = nc.dram_tensor("wvT", [D, D], F32, kind="ExternalInput").ap()
    bq = nc.dram_tensor("bq", [128, 8], F32, kind="ExternalInput").ap()
    bk = nc.dram_tensor("bk", [128, 8], F32, kind="ExternalInput").ap()
    bv = nc.dram_tensor("bv", [128, 8], F32, kind="ExternalInput").ap()

    o_x2 = nc.dram_tensor("o_x2", [D, T], F32, kind="ExternalOutput").ap()
    o_q = nc.dram_tensor("o_q", [D, T], BF16, kind="ExternalOutput").ap()
    o_k = nc.dram_tensor("o_k", [D, T], BF16, kind="ExternalOutput").ap()
    o_v = nc.dram_tensor("o_v", [D, T], BF16, kind="ExternalOutput").ap()
    o_r1 = nc.dram_tensor("o_r1", [2, 128, 8], F32, kind="ExternalOutput").ap()
    scr_x1 = nc.dram_tensor("scr_x1", [D, T], F32).ap()
    scr_ln = nc.dram_tensor("scr_ln", [D, T], BF16).ap()
    scr_l3 = nc.dram_tensor("scr_l3", [D, T], BF16).ap()

    with ExitStack() as st:
        sc = Sched(nc)
        stage = sbuf(st, nc, "stage", (128, 512), F32)
        ones_col = sbuf(st, nc, "ones_col", (128, 1), F32)
        ones_row = sbuf(st, nc, "ones_row", (1, 128), BF16)
        sc.op("vector", lambda h: h.memset(ones_col[:], 1.0))
        sc.op("vector", lambda h: h.memset(ones_row[:], 1.0))
        ln = LnKit(st, nc, "ln", CH)
        ps_mm = [psum_t(st, nc, f"ps_mm{i}", (128, CH)) for i in range(2)]
        ev1 = sbuf(st, nc, "ev1", (128, CH), F32)
        ev2 = sbuf(st, nc, "ev2", (128, CH), F32)
        ev3 = sbuf(st, nc, "ev3", (128, CH), F32)

        with ExitStack() as s1:
            wout_t = load_weight_bf16(sc, s1, nc, "wout", w_out, DI, D, stage)
            d_sb = sbuf(s1, nc, "d_sb", (128, 16), F32)
            dd = sc.op("sync", lambda h: h.dma_start(out=d_sb[:], in_=d_vec),
                       inc="yd", amt=16)
            xf = [sbuf(s1, nc, f"xf{k}", (128, CH), F32) for k in range(8)]
            xb = [sbuf(s1, nc, f"xb{k}", (128, CH), BF16) for k in range(8)]
            y_t = [sbuf(s1, nc, f"y{k}", (128, CH), BF16) for k in range(16)]
            r_t = [sbuf(s1, nc, f"r{k}", (128, CH), BF16) for k in range(16)]
            u_t = [sbuf(s1, nc, f"uu{k}", (128, CH), BF16) for k in range(16)]
            yg_t = [sbuf(s1, nc, f"yg{k}", (128, CH), BF16) for k in range(16)]
            for ci in range(NCH):
                c0 = ci * CH
                cb = sc.bar()
                dl = None
                for k in range(16):
                    dl = sc.op("sync", lambda h, k=k, c0=c0: h.dma_start(
                        out=y_t[k][:], in_=y_in[k * 128:(k + 1) * 128,
                                               c0:c0 + CH]),
                        waits=cb if k == 0 else (), inc="yd", amt=16)
                    dl = sc.op("sync", lambda h, k=k, c0=c0: h.dma_start(
                        out=r_t[k][:], in_=res_in[k * 128:(k + 1) * 128,
                                                  c0:c0 + CH]),
                        inc="yd", amt=16)
                    dl = sc.op("sync", lambda h, k=k, c0=c0: h.dma_start(
                        out=u_t[k][:], in_=u_in[k * 128:(k + 1) * 128,
                                                c0:c0 + CH]),
                        inc="yd", amt=16)
                dx = None
                for k in range(8):
                    dx = sc.op("sync", lambda h, k=k, c0=c0: h.dma_start(
                        out=xf[k][:], in_=xT[k * 128:(k + 1) * 128, c0:c0 + CH]),
                        inc="yd", amt=16)
                gdone = None
                for k in range(16):
                    sk = sc.op("scalar", lambda h, k=k: h.activation(
                        ev1[:], r_t[k][:], AF.Silu),
                        waits=(("yd", dl), ("v", sc.n("v"))))
                    sc.op("vector", lambda h, k=k: h.tensor_scalar(
                        ev2[:], u_t[k][:], d_sb[:, k:k + 1], None, AOP.mult),
                        waits=(("yd", dl), ("yd", dd)))
                    sc.op("vector", lambda h, k=k: h.tensor_add(
                        ev3[:], ev2[:], y_t[k][:]))
                    gdone = sc.op("vector", lambda h, k=k: h.tensor_mul(
                        yg_t[k][:], ev3[:], ev1[:]), waits=(("s", sk),))
                evd = {}
                for m in range(8):
                    pm = ps_mm[m % 2]
                    w0 = [("v", gdone)]
                    if m >= 2:
                        w0.append(("v", evd[m - 2]))
                    tm = emit_matmul_evac(sc, pm, wout_t, m * 128, yg_t, CH,
                                          16, tuple(w0))
                    sc.op("scalar", lambda h, pm=pm: h.copy(ev1[:], pm[:]),
                          waits=(("t", tm), ("v", sc.n("v"))))
                    sc.op("vector", lambda h, m=m: h.tensor_add(
                        ev2[:], ev1[:], xf[m][:]),
                        waits=(("yd", dx), ("s", sc.n("s"))))
                    evd[m] = sc.op("vector", lambda h, m=m: h.tensor_copy(
                        xf[m][:], ev2[:]))
                lnd = emit_ln(sc, ln, xf, 8, CH, xb, ones_col, ones_row,
                              waits0=(("v", evd[7]),))
                gx = None
                for k in range(8):
                    gx = sc.op("gpsimd", lambda h, k=k, c0=c0: h.dma_start(
                        out=scr_x1[k * 128:(k + 1) * 128, c0:c0 + CH],
                        in_=xf[k][:]), waits=(("v", lnd),), inc="gd", amt=16)
                    gx = sc.op("gpsimd", lambda h, k=k, c0=c0: h.dma_start(
                        out=scr_ln[k * 128:(k + 1) * 128, c0:c0 + CH],
                        in_=xb[k][:]), waits=(("v", lnd),), inc="gd", amt=16)
                sc.op("vector", lambda h: h.engine_nop(), waits=(("gd", gx),))
        with ExitStack() as s2:
            emit_ffn_section(sc, nc, s2, stage, ln, ps_mm, ev1, ones_col,
                             ones_row, w1, b1, w2, b2, scr_ln, scr_x1,
                             o_x2, scr_l3)
        with ExitStack() as s3:
            wq_t = load_weight_bf16(sc, s3, nc, "wq", wqT, D, D, stage)
            wk_t = load_weight_bf16(sc, s3, nc, "wk", wkT, D, D, stage)
            wv_t = load_weight_bf16(sc, s3, nc, "wv", wvT, D, D, stage)
            bq_sb = sbuf(s3, nc, "bq_sb", (128, 8), F32)
            bk_sb = sbuf(s3, nc, "bk_sb", (128, 8), F32)
            bv_sb = sbuf(s3, nc, "bv_sb", (128, 8), F32)
            for bb, src in [(bq_sb, bq), (bk_sb, bk), (bv_sb, bv)]:
                sc.op("sync", lambda h, bb=bb, src=src: h.dma_start(
                    out=bb[:], in_=src), inc="yd", amt=16)
            dbias = sc.n("yd")
            xb3 = [sbuf(s3, nc, f"xb3_{k}", (128, CH), BF16) for k in range(8)]
            pb = [sbuf(s3, nc, f"pb{i}", (128, CH), BF16) for i in range(2)]
            rq = [sbuf(s3, nc, f"rq{m}", (128, NCH), F32) for m in range(8)]
            rk = [sbuf(s3, nc, f"rk{m}", (128, NCH), F32) for m in range(8)]
            racc = sbuf(s3, nc, "racc", (128, 16), F32)
            for ci in range(NCH):
                c0 = ci * CH
                cb = sc.bar()
                dl = None
                for k in range(8):
                    dl = sc.op("sync", lambda h, k=k, c0=c0: h.dma_start(
                        out=xb3[k][:], in_=scr_l3[k * 128:(k + 1) * 128,
                                              c0:c0 + CH]),
                        waits=cb if k == 0 else (), inc="yd", amt=16)
                pdone = {}
                for wi, (wt, bsb, oo, sig, racc_t) in enumerate([
                        (wq_t, bq_sb, o_q, True, rq),
                        (wk_t, bk_sb, o_k, True, rk),
                        (wv_t, bv_sb, o_v, False, None)]):
                    for m in range(8):
                        pm = ps_mm[m % 2]
                        idx = wi * 8 + m
                        w0 = [("yd", dl), ("yd", dbias)]
                        if idx >= 2:
                            w0.append(("s", pdone[idx - 2]))
                        tm = emit_matmul_evac(sc, pm, wt, m * 128, xb3, CH, 8,
                                              tuple(w0))
                        par = idx % 2
                        wpb = (("t", tm), ("gd", sc.n("gd")),
                               ("v", sc.n("v")))
                        if sig:
                            pdone[idx] = sc.op("scalar",
                                lambda h, pm=pm, m=m, bsb=bsb, par=par:
                                h.activation(pb[par][:], pm[:], AF.Sigmoid,
                                             bias=bsb[:, m:m + 1]), waits=wpb)
                        else:
                            pdone[idx] = sc.op("scalar",
                                lambda h, pm=pm, m=m, bsb=bsb, par=par:
                                h.add(pb[par][:], pm[:], bsb[:, m:m + 1]),
                                waits=wpb)
                        sc.op("gpsimd", lambda h, oo=oo, m=m, c0=c0, par=par:
                              h.dma_start(
                                  out=oo[m * 128:(m + 1) * 128, c0:c0 + CH],
                                  in_=pb[par][:]),
                              waits=(("s", pdone[idx]),), inc="gd", amt=16)
                        if racc_t is not None:
                            sc.op("vector",
                                  lambda h, m=m, ci=ci, racc_t=racc_t, par=par:
                                  h.reduce_sum(racc_t[m][:, ci:ci + 1],
                                               pb[par][:],
                                               mybir.AxisListType.X),
                                  waits=(("s", pdone[idx]),))
            for m in range(8):
                sc.op("vector", lambda h, m=m: h.reduce_sum(
                    racc[:, 8 + m:9 + m], rk[m][:], mybir.AxisListType.X))
                sc.op("vector", lambda h, m=m: h.reduce_sum(
                    racc[:, m:m + 1], rq[m][:], mybir.AxisListType.X))
            gr = sc.op("gpsimd", lambda h: h.dma_start(
                out=o_r1.rearrange("a p b -> p a b"),
                in_=racc.ap().rearrange("p (a b) -> p a b", a=2)),
                waits=(("v", sc.n("v")),), inc="gd", amt=16)
            sc.op("gpsimd", lambda h: h.engine_nop(),
                  waits=(("gd", sc.n("gd")), ("yd", sc.n("yd"))), inc=None)
        sc.emit(st)
    return nc


def _prep_L3_inputs(l1_results, l2_results, inputs, core):
    g = (core // 4) * 4
    cr = core % 4
    x = np.asarray(inputs["L0_attn_a_x"], np.float32).reshape(B * L, D)
    y = np.concatenate([np.asarray(l2_results[g + j]["o_y"])[cr]
                        for j in range(4)], axis=0)
    u = np.concatenate([np.asarray(l1_results[core]["o_du"])[p, 1]
                        for p in range(4)], axis=0)

    def col16(v):
        return np.ascontiguousarray(
            np.asarray(v, np.float32).reshape(-1, 128).T)

    return {
        "xT": np.ascontiguousarray(x[core * T:(core + 1) * T].T),
        "y_in": y,
        "res_in": np.asarray(l1_results[core]["o_res"]),
        "u_in": u,
        "d_vec": col16(inputs["L0_attn_a_D"]),
        "w_out": np.asarray(inputs["L0_attn_a_W_out"], np.float32),
        "w1": np.asarray(inputs["L0_ffn_w1"], np.float32)[core],
        "b1": col16(np.asarray(inputs["L0_ffn_b1"], np.float32)[core]),
        "w2": np.asarray(inputs["L0_ffn_w2"], np.float32)[core],
        "b2": col16(np.asarray(inputs["L0_ffn_b2"], np.float32)[core]),
        "wqT": np.ascontiguousarray(
            np.asarray(inputs["L0_attn_b_Wq"], np.float32).T),
        "wkT": np.ascontiguousarray(
            np.asarray(inputs["L0_attn_b_Wk"], np.float32).T),
        "wvT": np.ascontiguousarray(
            np.asarray(inputs["L0_attn_b_Wv"], np.float32).T),
        "bq": col16(inputs["L0_attn_b_bq"]),
        "bk": col16(inputs["L0_attn_b_bk"]),
        "bv": col16(inputs["L0_attn_b_bv"]),
    }


def emit_ffn_section(sc, nc, sec, stage, ln, ps_mm, ev1, ones_col, ones_row,
                     w1_ap, b1_ap, w2_ap, b2_ap, scr_ln_in, scr_res_in,
                     out_ap, ln_scr_out):
    w1_t = load_weight_bf16(sc, sec, nc, "w1", w1_ap, D, DFF, stage)
    w2_t = load_weight_bf16(sc, sec, nc, "w2", w2_ap, DFF, D, stage)
    b1_sb = sbuf(sec, nc, "b1_sb", (128, 32), F32)
    b2_sb = sbuf(sec, nc, "b2_sb", (128, 8), F32)
    sc.op("sync", lambda h: h.dma_start(out=b1_sb[:], in_=b1_ap),
          inc="yd", amt=16)
    db = sc.op("sync", lambda h: h.dma_start(out=b2_sb[:], in_=b2_ap),
               inc="yd", amt=16)
    xb2 = [sbuf(sec, nc, f"xb2_{k}", (128, SQ), BF16) for k in range(8)]
    x2f = [sbuf(sec, nc, f"x2f{k}", (128, SQ), F32) for k in range(8)]
    x2b = [sbuf(sec, nc, f"x2b{k}", (128, SQ), BF16) for k in range(8)]
    h1_t = [sbuf(sec, nc, f"h1_{m}", (128, SQ), BF16) for m in range(32)]
    x1m = [sbuf(sec, nc, f"x1m{i}", (128, SQ), F32) for i in range(2)]
    xh = [sbuf(sec, nc, f"xh{i}", (128, SQ), F32) for i in range(2)]
    g1 = [sbuf(sec, nc, f"g1{i}", (128, SQ), F32) for i in range(2)]
    g2 = [sbuf(sec, nc, f"g2{i}", (128, SQ), F32) for i in range(2)]
    for ci in range(NSQ):
        c0 = ci * SQ
        cb = sc.bar()
        dl = None
        for k in range(8):
            dl = sc.op("sync", lambda h, k=k, c0=c0: h.dma_start(
                out=xb2[k][:], in_=scr_ln_in[k * 128:(k + 1) * 128,
                                             c0:c0 + SQ]),
                waits=cb if k == 0 else (), inc="yd", amt=16)
        hdone = {}
        for m in range(32):
            pm = ps_mm[m % 2]
            pr = m % 2
            xhp, g1p, g2p = xh[pr], g1[pr], g2[pr]
            w0 = [("yd", dl)]
            if m >= 2:
                w0.append(("v", hdone[m - 2]))
            tm = emit_matmul_evac(sc, pm, w1_t, m * 128, xb2, SQ, 8, tuple(w0))
            sc.op("scalar", lambda h, pm=pm, m=m, xhp=xhp: h.add(
                xhp[0:128, 0:SQ], pm[0:128, 0:SQ], b1_sb[:, m:m + 1]),
                waits=(("t", tm), ("v", hdone.get(m - 2, 0))))
            sadd = sc.n("s")
            sc.op("vector", lambda h, xhp=xhp, g1p=g1p: h.tensor_mul(
                g1p[0:128, 0:SQ], xhp[0:128, 0:SQ], xhp[0:128, 0:SQ]),
                waits=(("s", sadd),))
            sc.op("vector", lambda h, g1p=g1p, g2p=g2p: h.tensor_scalar(
                g2p[0:128, 0:SQ], g1p[0:128, 0:SQ], GELU_B, GELU_A,
                AOP.mult, AOP.add))
            vt2 = sc.op("vector", lambda h, g1p=g1p, g2p=g2p, xhp=xhp:
                        h.tensor_mul(g1p[0:128, 0:SQ], g2p[0:128, 0:SQ],
                                     xhp[0:128, 0:SQ]))
            sg = sc.op("scalar", lambda h, g1p=g1p, g2p=g2p: h.activation(
                g2p[0:128, 0:SQ], g1p[0:128, 0:SQ], AF.Sigmoid),
                waits=(("v", vt2),))
            hdone[m] = sc.op("vector", lambda h, m=m, xhp=xhp, g2p=g2p:
                             h.tensor_mul(h1_t[m][:], xhp[0:128, 0:SQ],
                                          g2p[0:128, 0:SQ]),
                             waits=(("s", sg),))
        evd = {}
        for m in range(8):
            pm = ps_mm[m % 2]
            dxm = sc.op("sync", lambda h, m=m, c0=c0: h.dma_start(
                out=x1m[m % 2][:], in_=scr_res_in[m * 128:(m + 1) * 128,
                                                  c0:c0 + SQ]),
                waits=(("v", evd.get(m - 2, 0)),), inc="yd", amt=16)
            w0 = [("v", hdone[31])]
            if m >= 2:
                w0.append(("v", evd[m - 2]))
            tm = emit_matmul_evac(sc, pm, w2_t, m * 128, h1_t, SQ, 32,
                                  tuple(w0))
            sc.op("scalar", lambda h, pm=pm, m=m: h.add(
                ev1[0:128, 0:SQ], pm[0:128, 0:SQ], b2_sb[:, m:m + 1]),
                waits=(("t", tm), ("yd", db), ("v", sc.n("v"))))
            sc.op("vector", lambda h, m=m: h.tensor_add(
                x2f[m][:], ev1[0:128, 0:SQ], x1m[m % 2][:]),
                waits=(("s", sc.n("s")), ("yd", dxm)))
            evd[m] = sc.op("vector", lambda h, m=m: h.tensor_copy(
                x2b[m][:], x2f[m][:]))
            sc.op("gpsimd", lambda h, m=m, c0=c0: h.dma_start(
                out=out_ap[m * 128:(m + 1) * 128, c0:c0 + SQ],
                in_=x2f[m][:]), waits=(("v", evd[m]),), inc="gd", amt=16)
        if ln_scr_out is not None:
            lnd = emit_ln(sc, ln, x2f, 8, SQ, x2b, ones_col, ones_row,
                          waits0=(("v", evd[7]),))
            gl = None
            for k in range(8):
                gl = sc.op("gpsimd", lambda h, k=k, c0=c0: h.dma_start(
                    out=ln_scr_out[k * 128:(k + 1) * 128, c0:c0 + SQ],
                    in_=x2b[k][:]), waits=(("v", lnd),), inc="gd", amt=16)
            sc.op("vector", lambda h: h.engine_nop(), waits=(("gd", gl),))
        else:
            sc.op("vector", lambda h: h.engine_nop(),
                  waits=(("gd", sc.n("gd")),))


def emit_headblk_w(sc, nc, sec, tag, wsrc, blk, ones_bf, ps_s, sumw, esum):
    wf = sbuf(sec, nc, f"{tag}_wf{blk}", (128, 2), F32)
    wb = sbuf(sec, nc, f"{tag}_wb{blk}", (128, 2), BF16)
    sc.op("vector", lambda h: h.memset(wf[:], 0.0))
    sc.op("vector", lambda h: h.tensor_copy(
        wf[0:64, 0:1], wsrc[0:64, blk:blk + 1]))
    sc.op("vector", lambda h: h.tensor_copy(
        wf[64:128, 1:2], wsrc[64:128, blk:blk + 1]))
    vv = sc.op("vector", lambda h: h.tensor_copy(wb[:], wf[:]))
    tm = sc.op("tensor", lambda h: h.matmul(
        ps_s[0:2, 0:1], wb[:], ones_bf[:], start=True, stop=True),
        waits=(("v", vv),), inc="t")
    sc.op("vector", lambda h: h.tensor_copy(
        sumw[0:2, blk:blk + 1], ps_s[0:2, 0:1]), waits=(("t", tm),))
    sc.op("vector", lambda h: h.tensor_scalar_mul(
        esum[0:2, blk:blk + 1], sumw[0:2, blk:blk + 1], EPS_AT))
    return wb


def build_L4():
    nc = bass.Bass("TRN2", target_bir_lowering=False)
    q_in = nc.dram_tensor("q_in", [D, T], BF16, kind="ExternalInput").ap()
    k_in = nc.dram_tensor("k_in", [D, T], BF16, kind="ExternalInput").ap()
    r1all = nc.dram_tensor("r1all", [4, 2, 128, 8], F32, kind="ExternalInput").ap()
    o_r2 = nc.dram_tensor("o_r2", [2, 128, 8], F32, kind="ExternalOutput").ap()
    o_si = nc.dram_tensor("o_si", [8, 2, T], F32, kind="ExternalOutput").ap()
    o_so = nc.dram_tensor("o_so", [8, 2, T], F32, kind="ExternalOutput").ap()
    selh_in = nc.dram_tensor("selh_in", [2, 128], F32, kind="ExternalInput").ap()

    with ExitStack() as st:
        sc = Sched(nc)
        ones_bf = sbuf(st, nc, "ones_bf", (128, 1), BF16)
        selh = sbuf(st, nc, "selh", (2, 128), BF16)
        selhf = sbuf(st, nc, "selhf", (2, 128), F32)
        sc.op("vector", lambda h: h.memset(ones_bf[:], 1.0))
        dsel = sc.op("sync", lambda h: h.dma_start(out=selhf[:], in_=selh_in),
                     inc="yd", amt=16)
        sc.op("vector", lambda h: h.tensor_copy(selh[:], selhf[:]),
              waits=(("yd", dsel),))
        r1sb = sbuf(st, nc, "r1sb", (128, 64), F32)
        d0 = sc.op("sync", lambda h: h.dma_start(
            out=r1sb.ap().rearrange("p (j a b) -> p j a b", j=4, a=2),
            in_=r1all.rearrange("j a p b -> p j a b")), inc="yd", amt=16)
        wq_s = sbuf(st, nc, "wq_s", (128, 8), F32)
        wk_s = sbuf(st, nc, "wk_s", (128, 8), F32)
        tmp8 = sbuf(st, nc, "tmp8", (128, 8), F32)
        tmp8b = sbuf(st, nc, "tmp8b", (128, 8), F32)
        tmp8c = sbuf(st, nc, "tmp8c", (128, 8), F32)
        v = r1sb.ap().rearrange("p (j a b) -> p j a b", j=4, a=2)
        for (dst, a) in [(wq_s, 0), (wk_s, 1)]:
            sc.op("vector", lambda h, a=a: h.tensor_add(
                tmp8[:], v[:, 0, a, :], v[:, 1, a, :]),
                waits=(("yd", d0),))
            sc.op("vector", lambda h, a=a: h.tensor_add(
                tmp8b[:], v[:, 2, a, :], v[:, 3, a, :]))
            sc.op("vector", lambda h: h.tensor_add(
                tmp8c[:], tmp8[:], tmp8b[:]))
            sc.op("vector", lambda h, dst=dst: h.tensor_scalar_add(
                dst[:], tmp8c[:], EPS_AT))
        ps_s = psum_t(st, nc, "ps_s", (128, 512))
        sumwk = sbuf(st, nc, "sumwk", (2, 8), F32)
        esk = sbuf(st, nc, "esk", (2, 8), F32)
        sumwq = sbuf(st, nc, "sumwq", (2, 8), F32)
        esq = sbuf(st, nc, "esq", (2, 8), F32)
        wkb = [emit_headblk_w(sc, nc, st, "k", wk_s, b, ones_bf, ps_s,
                              sumwk, esk) for b in range(8)]
        wqb = [emit_headblk_w(sc, nc, st, "q", wq_s, b, ones_bf, ps_s,
                              sumwq, esq) for b in range(8)]

        qt = [sbuf(st, nc, f"qt{k}", (128, CH), BF16) for k in range(8)]
        kt = [sbuf(st, nc, f"kt{k}", (128, CH), BF16) for k in range(8)]
        dn = sbuf(st, nc, "dn", (2, CH), F32)
        dn2 = sbuf(st, nc, "dn2", (2, CH), F32)
        sxf = sbuf(st, nc, "sxf", (2, CH), F32)
        sxb = sbuf(st, nc, "sxb", (2, CH), BF16)
        srep = sbuf(st, nc, "srep", (128, CH), BF16)
        prod = sbuf(st, nc, "prod", (128, CH), F32)
        rqs = [sbuf(st, nc, f"rqs{m}", (128, NCH), F32) for m in range(8)]
        rks = [sbuf(st, nc, f"rks{m}", (128, NCH), F32) for m in range(8)]
        racc = sbuf(st, nc, "racc", (128, 16), F32)
        ps_d = psum_t(st, nc, "ps_d", (128, 512))
        ps_r = psum_t(st, nc, "ps_r", (128, 512))
        for ci in range(NCH):
            c0 = ci * CH
            cb = sc.bar()
            dl = None
            for k in range(8):
                dl = sc.op("sync", lambda h, k=k, c0=c0: h.dma_start(
                    out=qt[k][:], in_=q_in[k * 128:(k + 1) * 128, c0:c0 + CH]),
                    waits=cb if k == 0 else (), inc="yd", amt=16)
                dl = sc.op("sync", lambda h, k=k, c0=c0: h.dma_start(
                    out=kt[k][:], in_=k_in[k * 128:(k + 1) * 128, c0:c0 + CH]),
                    inc="yd", amt=16)
            for blk in range(8):
                for (wb, xt, es, oo, racc_t) in [
                        (wkb[blk], qt[blk], esk, o_si, rqs),
                        (wqb[blk], kt[blk], esq, o_so, rks)]:
                    tm = sc.op("tensor", lambda h, wb=wb, xt=xt: h.matmul(
                        ps_d[0:2, 0:CH], wb[:], xt[:], start=True, stop=True),
                        waits=(("yd", dl), ("v", sc.n("v"))), inc="t")
                    sc.op("vector", lambda h: h.tensor_copy(
                        dn[0:2, 0:CH], ps_d[0:2, 0:CH]), waits=(("t", tm),))
                    sc.op("vector", lambda h, es=es, blk=blk: h.tensor_scalar(
                        dn2[0:2, 0:CH], dn[0:2, 0:CH], es[0:2, blk:blk + 1],
                        None, AOP.add))
                    sc.op("vector", lambda h: h.reciprocal(
                        sxf[0:2, 0:CH], dn2[0:2, 0:CH]))
                    vv = sc.op("vector", lambda h: h.tensor_copy(
                        sxb[0:2, 0:CH], sxf[0:2, 0:CH]))
                    gsi = sc.op("gpsimd", lambda h, oo=oo, blk=blk, c0=c0:
                                h.dma_start(out=oo[blk, :, c0:c0 + CH],
                                            in_=sxf[0:2, 0:CH]),
                                waits=(("v", vv),), inc="gd", amt=16)
                    tm2 = sc.op("tensor", lambda h: h.matmul(
                        ps_r[0:128, 0:CH], selh[:], sxb[0:2, 0:CH],
                        start=True, stop=True), waits=(("v", vv),), inc="t")
                    sc.op("vector", lambda h: h.tensor_copy(
                        srep[:], ps_r[0:128, 0:CH]),
                        waits=(("t", tm2), ("gd", gsi)))
                    sc.op("vector", lambda h, xt=xt: h.tensor_mul(
                        prod[:], xt[:], srep[:]))
                    sc.op("vector", lambda h, racc_t=racc_t, blk=blk, ci=ci:
                          h.reduce_sum(racc_t[blk][:, ci:ci + 1], prod[:],
                                       mybir.AxisListType.X))
        for m in range(8):
            sc.op("vector", lambda h, m=m: h.reduce_sum(
                racc[:, m:m + 1], rqs[m][:], mybir.AxisListType.X))
            sc.op("vector", lambda h, m=m: h.reduce_sum(
                racc[:, 8 + m:9 + m], rks[m][:], mybir.AxisListType.X))
        gr = sc.op("gpsimd", lambda h: h.dma_start(
            out=o_r2.rearrange("a p b -> p a b"),
            in_=racc.ap().rearrange("p (a b) -> p a b", a=2)),
            waits=(("v", sc.n("v")),), inc="gd", amt=16)
        sc.op("gpsimd", lambda h: h.engine_nop(),
              waits=(("gd", sc.n("gd")), ("yd", sc.n("yd"))), inc=None)
        sc.emit(st)
    return nc


def build_L5():
    nc = bass.Bass("TRN2", target_bir_lowering=False)
    q_in = nc.dram_tensor("q_in", [D, T], BF16, kind="ExternalInput").ap()
    k_in = nc.dram_tensor("k_in", [D, T], BF16, kind="ExternalInput").ap()
    v_in = nc.dram_tensor("v_in", [D, T], BF16, kind="ExternalInput").ap()
    r2all = nc.dram_tensor("r2all", [4, 2, 128, 8], F32, kind="ExternalInput").ap()
    ident = nc.dram_tensor("ident", [128, 128], F32, kind="ExternalInput").ap()
    o_cs = nc.dram_tensor("o_cs", [8, 2, T], F32, kind="ExternalOutput").ap()
    o_z = nc.dram_tensor("o_z", [2, 8], F32, kind="ExternalOutput").ap()
    o_kvp = nc.dram_tensor("o_kvp", [128, 1024], F32, kind="ExternalOutput").ap()

    with ExitStack() as st:
        sc = Sched(nc)
        ones_bf = sbuf(st, nc, "ones_bf", (128, 1), BF16)
        sc.op("vector", lambda h: h.memset(ones_bf[:], 1.0))
        stage = sbuf(st, nc, "stage", (128, 128), F32)
        id_bf = sbuf(st, nc, "id_bf", (128, 128), BF16)
        dd = sc.op("sync", lambda h: h.dma_start(out=stage[:], in_=ident),
                   inc="yd", amt=16)
        sc.op("vector", lambda h: h.tensor_copy(id_bf[:], stage[:]),
              waits=(("yd", dd),))
        r2sb = sbuf(st, nc, "r2sb", (128, 64), F32)
        d0 = sc.op("sync", lambda h: h.dma_start(
            out=r2sb.ap().rearrange("p (j a b) -> p j a b", j=4, a=2),
            in_=r2all.rearrange("j a p b -> p j a b")), inc="yd", amt=16)
        wqsi = sbuf(st, nc, "wqsi", (128, 8), F32)
        wkso = sbuf(st, nc, "wkso", (128, 8), F32)
        t8a = sbuf(st, nc, "t8a", (128, 8), F32)
        t8b = sbuf(st, nc, "t8b", (128, 8), F32)
        t8c = sbuf(st, nc, "t8c", (128, 8), F32)
        v8 = r2sb.ap().rearrange("p (j a b) -> p j a b", j=4, a=2)
        for (dst, a) in [(wqsi, 0), (wkso, 1)]:
            sc.op("vector", lambda h, a=a: h.tensor_add(
                t8a[:], v8[:, 0, a, :], v8[:, 1, a, :]), waits=(("yd", d0),))
            sc.op("vector", lambda h, a=a: h.tensor_add(
                t8b[:], v8[:, 2, a, :], v8[:, 3, a, :]))
            sc.op("vector", lambda h: h.tensor_add(t8c[:], t8a[:], t8b[:]))
            sc.op("vector", lambda h, dst=dst: h.tensor_scalar_add(
                dst[:], t8c[:], EPS_AT))
        ps_s = psum_t(st, nc, "ps_s", (128, 512))
        sum_ks = sbuf(st, nc, "sum_ks", (2, 8), F32)
        e_ks = sbuf(st, nc, "e_ks", (2, 8), F32)
        sum_qs = sbuf(st, nc, "sum_qs", (2, 8), F32)
        e_qs = sbuf(st, nc, "e_qs", (2, 8), F32)
        wksob = [emit_headblk_w(sc, nc, st, "ks", wkso, b, ones_bf, ps_s,
                                sum_ks, e_ks) for b in range(8)]
        wqsib = [emit_headblk_w(sc, nc, st, "qs", wqsi, b, ones_bf, ps_s,
                                sum_qs, e_qs) for b in range(8)]

        qt = [sbuf(st, nc, f"qt{k}", (128, CH), BF16) for k in range(8)]
        kt = [sbuf(st, nc, f"kt{k}", (128, CH), BF16) for k in range(8)]
        vt = [sbuf(st, nc, f"vt{k}", (128, CH), BF16) for k in range(8)]
        dn = sbuf(st, nc, "dn", (2, CH), F32)
        dn2 = sbuf(st, nc, "dn2", (2, CH), F32)
        csf = sbuf(st, nc, "csf", (2, CH), F32)
        ebf = sbuf(st, nc, "ebf", (2, CH), BF16)
        e_all = sbuf(st, nc, "e_all", (16, CH), BF16)
        zacc = [sbuf(st, nc, f"zacc{m}", (2, NCH), F32) for m in range(8)]
        zfin = sbuf(st, nc, "zfin", (2, 8), F32)
        kT = sbuf(st, nc, "kT", (128, 1024), BF16)
        vT = sbuf(st, nc, "vT", (128, 1024), BF16)
        vscT = sbuf(st, nc, "vscT", (128, 1024), BF16)
        eT = sbuf(st, nc, "eT", (128, 16), F32)
        kv_ev = sbuf(st, nc, "kv_ev", (128, 1024), F32)
        ps_d = psum_t(st, nc, "ps_d", (128, 512))
        ps_t = psum_t(st, nc, "ps_t", (128, 1024), BF16)
        ps_e = psum_t(st, nc, "ps_e", (128, 1024), BF16)
        ps_kv = psum_t(st, nc, "ps_kv", (128, 1024))
        for ci in range(NCH):
            c0 = ci * CH
            cb = sc.bar()
            dl = None
            for k in range(8):
                dl = sc.op("sync", lambda h, k=k, c0=c0: h.dma_start(
                    out=qt[k][:], in_=q_in[k * 128:(k + 1) * 128, c0:c0 + CH]),
                    waits=cb if k == 0 else (), inc="yd", amt=16)
                dl = sc.op("sync", lambda h, k=k, c0=c0: h.dma_start(
                    out=kt[k][:], in_=k_in[k * 128:(k + 1) * 128, c0:c0 + CH]),
                    inc="yd", amt=16)
                dl = sc.op("sync", lambda h, k=k, c0=c0: h.dma_start(
                    out=vt[k][:], in_=v_in[k * 128:(k + 1) * 128, c0:c0 + CH]),
                    inc="yd", amt=16)
            for blk in range(8):
                tm = sc.op("tensor", lambda h, blk=blk: h.matmul(
                    ps_d[0:2, 0:CH], wksob[blk][:], qt[blk][:],
                    start=True, stop=True),
                    waits=(("yd", dl), ("v", sc.n("v"))), inc="t")
                sc.op("vector", lambda h: h.tensor_copy(
                    dn[0:2, 0:CH], ps_d[0:2, 0:CH]), waits=(("t", tm),))
                vcs = sc.op("vector", lambda h, blk=blk: h.tensor_scalar(
                    csf[0:2, 0:CH], dn[0:2, 0:CH], e_ks[0:2, blk:blk + 1],
                    None, AOP.add))
                gcs = sc.op("gpsimd", lambda h, blk=blk, c0=c0: h.dma_start(
                    out=o_cs[blk, :, c0:c0 + CH], in_=csf[0:2, 0:CH]),
                    waits=(("v", vcs),), inc="gd", amt=16)
                tm = sc.op("tensor", lambda h, blk=blk: h.matmul(
                    ps_d[0:2, 0:CH], wqsib[blk][:], kt[blk][:],
                    start=True, stop=True),
                    waits=(("v", sc.n("v")),), inc="t")
                sc.op("vector", lambda h: h.tensor_copy(
                    dn[0:2, 0:CH], ps_d[0:2, 0:CH]),
                    waits=(("t", tm), ("gd", gcs)))
                sc.op("vector", lambda h, blk=blk: h.tensor_scalar(
                    dn2[0:2, 0:CH], dn[0:2, 0:CH], e_qs[0:2, blk:blk + 1],
                    None, AOP.add))
                sc.op("vector", lambda h: h.tensor_scalar_min(
                    dn[0:2, 0:CH], dn2[0:2, 0:CH], 1.0))
                vcl = sc.op("vector", lambda h: h.tensor_scalar_max(
                    dn2[0:2, 0:CH], dn[0:2, 0:CH], -1.0))
                se = sc.op("scalar", lambda h: h.activation(
                    csf[0:2, 0:CH], dn2[0:2, 0:CH], AF.Exp),
                    waits=(("v", vcl),))
                sc.op("vector", lambda h, blk=blk, ci=ci: h.reduce_sum(
                    zacc[blk][:, ci:ci + 1], csf[0:2, 0:CH],
                    mybir.AxisListType.X), waits=(("s", se),))
                vv = sc.op("vector", lambda h: h.tensor_copy(
                    ebf[0:2, 0:CH], csf[0:2, 0:CH]))
                ge = sc.op("gpsimd", lambda h, blk=blk: h.dma_start(
                    out=e_all[2 * blk:2 * blk + 2, :], in_=ebf[0:2, 0:CH]),
                    waits=(("v", vv), ("t", sc.n("t"))), inc="gd", amt=16)
                sc.op("vector", lambda h: h.engine_nop(), waits=(("gd", ge),))
            for tch in range(CH // 128):
                tc = tch * 128
                for k in range(8):
                    tm = sc.op("tensor", lambda h, k=k, tc=tc: h.transpose(
                        ps_t[0:128, 0:128], kt[k][:, tc:tc + 128], id_bf[:]),
                        waits=(("v", sc.n("v")),), inc="t")
                    sc.op("vector", lambda h, k=k: h.tensor_copy(
                        kT[:, k * 128:(k + 1) * 128], ps_t[0:128, 0:128]),
                        waits=(("t", tm),))
                    tm = sc.op("tensor", lambda h, k=k, tc=tc: h.transpose(
                        ps_t[0:128, 0:128], vt[k][:, tc:tc + 128], id_bf[:]),
                        waits=(("v", sc.n("v")),), inc="t")
                    sc.op("vector", lambda h, k=k: h.tensor_copy(
                        vT[:, k * 128:(k + 1) * 128], ps_t[0:128, 0:128]),
                        waits=(("t", tm),))
                tm = sc.op("tensor", lambda h, tc=tc: h.transpose(
                    ps_e[0:128, 0:16], e_all[0:16, tc:tc + 128], id_bf[0:16, 0:16]),
                    waits=(("v", sc.n("v")),), inc="t")
                sc.op("vector", lambda h: h.tensor_copy(
                    eT[:, 0:16], ps_e[0:128, 0:16]), waits=(("t", tm),))
                for hh in range(16):
                    sc.op("vector", lambda h, hh=hh: h.tensor_scalar(
                        vscT[:, hh * 64:(hh + 1) * 64],
                        vT[:, hh * 64:(hh + 1) * 64],
                        eT[:, hh:hh + 1], None, AOP.mult),
                        waits=(("t", sc.n("t")),) if hh == 0 else ())
                vsc_done = sc.n("v")
                for pp in range(8):
                    sc.op("tensor", lambda h, pp=pp, ci=ci, tch=tch: h.matmul(
                        ps_kv[:, pp * 128:(pp + 1) * 128],
                        kT[:, pp * 128:(pp + 1) * 128],
                        vscT[:, pp * 128:(pp + 1) * 128],
                        start=(ci == 0 and tch == 0),
                        stop=(ci == NCH - 1 and tch == CH // 128 - 1)),
                        waits=(("v", vsc_done),), inc="t")
        tkv = sc.n("t")
        vv = sc.op("vector", lambda h: h.tensor_copy(kv_ev[:], ps_kv[:]),
                   waits=(("t", tkv),))
        sc.op("gpsimd", lambda h: h.dma_start(out=o_kvp, in_=kv_ev[:]),
              waits=(("v", vv),), inc="gd", amt=16)
        for m in range(8):
            sc.op("vector", lambda h, m=m: h.reduce_sum(
                zfin[:, m:m + 1], zacc[m][:], mybir.AxisListType.X))
        vz = sc.n("v")
        sc.op("gpsimd", lambda h: h.dma_start(out=o_z, in_=zfin[0:2, 0:8]),
              waits=(("v", vz),), inc="gd", amt=16)
        sc.op("gpsimd", lambda h: h.engine_nop(),
              waits=(("gd", sc.n("gd")), ("yd", sc.n("yd"))), inc=None)
        sc.emit(st)
    return nc


def build_L6():
    nc = bass.Bass("TRN2", target_bir_lowering=False)
    q_in = nc.dram_tensor("q_in", [D, T], BF16, kind="ExternalInput").ap()
    si_in = nc.dram_tensor("si_in", [8, 2, T], F32, kind="ExternalInput").ap()
    cs_in = nc.dram_tensor("cs_in", [8, 2, T], F32, kind="ExternalInput").ap()
    zall = nc.dram_tensor("zall", [4, 2, 8], F32, kind="ExternalInput").ap()
    kvall = nc.dram_tensor("kvall", [4, 128, 1024], F32, kind="ExternalInput").ap()
    x2_in = nc.dram_tensor("x2_in", [D, T], F32, kind="ExternalInput").ap()
    woT = nc.dram_tensor("woT", [D, D], F32, kind="ExternalInput").ap()
    bo = nc.dram_tensor("bo", [128, 8], F32, kind="ExternalInput").ap()
    w1 = nc.dram_tensor("w1", [D, DFF], F32, kind="ExternalInput").ap()
    b1 = nc.dram_tensor("b1", [128, 32], F32, kind="ExternalInput").ap()
    w2 = nc.dram_tensor("w2", [DFF, D], F32, kind="ExternalInput").ap()
    b2 = nc.dram_tensor("b2", [128, 8], F32, kind="ExternalInput").ap()
    o_fin = nc.dram_tensor("o_fin", [D, T], F32, kind="ExternalOutput").ap()
    scr_x3 = nc.dram_tensor("scr_x3", [D, T], F32).ap()
    scr_l4 = nc.dram_tensor("scr_l4", [D, T], BF16).ap()
    selh_in = nc.dram_tensor("selh_in", [2, 128], F32, kind="ExternalInput").ap()

    with ExitStack() as st:
        sc = Sched(nc)
        stage = sbuf(st, nc, "stage", (128, 512), F32)
        ones_col = sbuf(st, nc, "ones_col", (128, 1), F32)
        ones_row = sbuf(st, nc, "ones_row", (1, 128), BF16)
        selh = sbuf(st, nc, "selh", (2, 128), BF16)
        selhf = sbuf(st, nc, "selhf", (2, 128), F32)
        sc.op("vector", lambda h: h.memset(ones_col[:], 1.0))
        sc.op("vector", lambda h: h.memset(ones_row[:], 1.0))
        dsel = sc.op("sync", lambda h: h.dma_start(out=selhf[:], in_=selh_in),
                     inc="yd", amt=16)
        sc.op("vector", lambda h: h.tensor_copy(selh[:], selhf[:]),
              waits=(("yd", dsel),))
        ln = LnKit(st, nc, "ln", CH)
        ps_mm = [psum_t(st, nc, f"ps_mm{i}", (128, CH)) for i in range(2)]
        ev1 = sbuf(st, nc, "ev1", (128, CH), F32)

        with ExitStack() as s1:
            kv4 = sbuf(s1, nc, "kv4", (128, 1024), F32)
            kvp = [sbuf(s1, nc, f"kvp{i}", (128, 1024), F32) for i in range(2)]
            kvb = sbuf(s1, nc, "kvb", (128, 1024), BF16)
            d0 = sc.op("sync", lambda h: h.dma_start(
                out=kvp[0][:], in_=kvall[0]), inc="yd", amt=16)
            for j in range(1, 4):
                dj = sc.op("sync", lambda h, j=j: h.dma_start(
                    out=kv4[:], in_=kvall[j]),
                    waits=(("v", sc.n("v")),), inc="yd", amt=16)
                sc.op("vector", lambda h, j=j: h.tensor_add(
                    kvp[j % 2][:], kvp[(j + 1) % 2][:], kv4[:]),
                    waits=(("yd", dj),))
            vkv = sc.op("vector", lambda h: h.tensor_copy(kvb[:], kvp[1][:]))
            for pp in range(8):
                sc.op("vector", lambda h, pp=pp: h.memset(
                    kvb[0:64, pp * 128 + 64:pp * 128 + 128], 0.0))
                sc.op("vector", lambda h, pp=pp: h.memset(
                    kvb[64:128, pp * 128:pp * 128 + 64], 0.0))
            z4 = sbuf(s1, nc, "z4", (2, 32), F32)
            zg = sbuf(s1, nc, "zg", (2, 8), F32)
            zi = sbuf(s1, nc, "zi", (2, 8), F32)
            dz = sc.op("sync", lambda h: h.dma_start(
                out=z4.ap().rearrange("a (j b) -> a j b", j=4),
                in_=zall.rearrange("j a b -> a j b")), inc="yd", amt=16)
            zt = sbuf(s1, nc, "zt", (2, 8), F32)
            sc.op("vector", lambda h: h.tensor_add(
                zg[:], z4[:, 0:8], z4[:, 8:16]), waits=(("yd", dz),))
            sc.op("vector", lambda h: h.tensor_add(
                zi[:], z4[:, 16:24], z4[:, 24:32]))
            sc.op("vector", lambda h: h.tensor_add(zt[:], zg[:], zi[:]))
            sc.op("vector", lambda h: h.reciprocal(zg[:], zt[:]))
            sc.op("vector", lambda h: h.tensor_scalar_mul(
                zi[:], zg[:], float(L)))
            wo_t = load_weight_bf16(sc, s1, nc, "wo", woT, D, D, stage)
            bo_sb = sbuf(s1, nc, "bo_sb", (128, 8), F32)
            sc.op("sync", lambda h: h.dma_start(out=bo_sb[:], in_=bo),
                  inc="yd", amt=16)
            qt = [sbuf(s1, nc, f"qt{k}", (128, CH), BF16) for k in range(8)]
            qs_t = [sbuf(s1, nc, f"qs{k}", (128, CH), BF16) for k in range(8)]
            ao_t = [sbuf(s1, nc, f"ao{k}", (128, CH), BF16) for k in range(8)]
            xf = [sbuf(s1, nc, f"xf{k}", (128, CH), F32) for k in range(8)]
            xb = [sbuf(s1, nc, f"xb{k}", (128, CH), BF16) for k in range(8)]
            sif = sbuf(s1, nc, "sif", (2, CH), F32)
            sib = sbuf(s1, nc, "sib", (2, CH), BF16)
            saf = sbuf(s1, nc, "saf", (2, CH), F32)
            sab = sbuf(s1, nc, "sab", (2, CH), BF16)
            srep = sbuf(s1, nc, "srep", (128, CH), BF16)
            ps_r = psum_t(s1, nc, "ps_r", (128, 512))
            for ci in range(NCH):
                c0 = ci * CH
                cb = sc.bar()
                dl = None
                for k in range(8):
                    dl = sc.op("sync", lambda h, k=k, c0=c0: h.dma_start(
                        out=qt[k][:], in_=q_in[k * 128:(k + 1) * 128,
                                               c0:c0 + CH]),
                        waits=cb if k == 0 else (), inc="yd", amt=16)
                dsi = {}
                for blk in range(8):
                    dsi[blk] = sc.op("sync", lambda h, blk=blk, c0=c0:
                                     h.dma_start(out=sif[0:2, 0:CH],
                                                 in_=si_in[blk, :, c0:c0 + CH]),
                                     waits=(("v", sc.n("v")),),
                                     inc="yd", amt=16)
                    vv = sc.op("vector", lambda h: h.tensor_copy(
                        sib[0:2, 0:CH], sif[0:2, 0:CH]),
                        waits=(("yd", dsi[blk]),))
                    tm = sc.op("tensor", lambda h: h.matmul(
                        ps_r[0:128, 0:CH], selh[:], sib[0:2, 0:CH],
                        start=True, stop=True), waits=(("v", vv),), inc="t")
                    sc.op("vector", lambda h: h.tensor_copy(
                        srep[:], ps_r[0:128, 0:CH]), waits=(("t", tm),))
                    sc.op("vector", lambda h, blk=blk: h.tensor_mul(
                        qs_t[blk][:], qt[blk][:], srep[:]))
                for pp in range(8):
                    tm = sc.op("tensor", lambda h, pp=pp: h.matmul(
                        ps_mm[pp % 2][:], kvb[:, pp * 128:(pp + 1) * 128],
                        qs_t[pp][:], start=True, stop=True),
                        waits=(("v", sc.n("v")),), inc="t")
                    dcs = sc.op("sync", lambda h, pp=pp, c0=c0: h.dma_start(
                        out=saf[0:2, 0:CH], in_=cs_in[pp, :, c0:c0 + CH]),
                        waits=(("v", sc.n("v")),), inc="yd", amt=16)
                    ss = sc.op("scalar", lambda h: h.activation(
                        sif[0:2, 0:CH], saf[0:2, 0:CH], AF.Sigmoid),
                        waits=(("yd", dcs), ("v", sc.n("v"))))
                    sc.op("vector", lambda h, pp=pp: h.tensor_scalar(
                        sab[0:2, 0:CH], sif[0:2, 0:CH], zi[0:2, pp:pp + 1],
                        None, AOP.mult), waits=(("s", ss),))
                    tm2 = sc.op("tensor", lambda h: h.matmul(
                        ps_r[0:128, 0:CH], selh[:], sab[0:2, 0:CH],
                        start=True, stop=True),
                        waits=(("v", sc.n("v")),), inc="t")
                    sc.op("vector", lambda h: h.tensor_copy(
                        srep[:], ps_r[0:128, 0:CH]), waits=(("t", tm2),))
                    sc.op("vector", lambda h, pp=pp: h.tensor_copy(
                        ev1[:], ps_mm[pp % 2][:]), waits=(("t", tm),))
                    sc.op("vector", lambda h, pp=pp: h.tensor_mul(
                        ao_t[pp][:], ev1[:], srep[:]))
                aodone = sc.n("v")
                dx = None
                for k in range(8):
                    dx = sc.op("sync", lambda h, k=k, c0=c0: h.dma_start(
                        out=xf[k][:], in_=x2_in[k * 128:(k + 1) * 128,
                                                c0:c0 + CH]),
                        waits=(("v", sc.n("v")),), inc="yd", amt=16)
                evd = {}
                for m in range(8):
                    pm = ps_mm[m % 2]
                    w0 = [("v", aodone)]
                    if m >= 2:
                        w0.append(("v", evd[m - 2]))
                    tm = emit_matmul_evac(sc, pm, wo_t, m * 128, ao_t, CH, 8,
                                          tuple(w0))
                    sc.op("scalar", lambda h, pm=pm, m=m: h.add(
                        ev1[:], pm[:], bo_sb[:, m:m + 1]),
                        waits=(("t", tm), ("v", sc.n("v"))))
                    sc.op("vector", lambda h, m=m: h.tensor_add(
                        ln.tmp[0][0:128, 0:CH], ev1[:], xf[m][:]),
                        waits=(("s", sc.n("s")), ("yd", dx)))
                    evd[m] = sc.op("vector", lambda h, m=m: h.tensor_copy(
                        xf[m][:], ln.tmp[0][0:128, 0:CH]))
                lnd = emit_ln(sc, ln, xf, 8, CH, xb, ones_col, ones_row,
                              waits0=(("v", evd[7]),))
                gx = None
                for k in range(8):
                    gx = sc.op("gpsimd", lambda h, k=k, c0=c0: h.dma_start(
                        out=scr_x3[k * 128:(k + 1) * 128, c0:c0 + CH],
                        in_=xf[k][:]), waits=(("v", lnd),), inc="gd", amt=16)
                    gx = sc.op("gpsimd", lambda h, k=k, c0=c0: h.dma_start(
                        out=scr_l4[k * 128:(k + 1) * 128, c0:c0 + CH],
                        in_=xb[k][:]), waits=(("v", lnd),), inc="gd", amt=16)
                sc.op("vector", lambda h: h.engine_nop(), waits=(("gd", gx),))
        with ExitStack() as s2:
            emit_ffn_section(sc, nc, s2, stage, ln, ps_mm, ev1, ones_col,
                             ones_row, w1, b1, w2, b2, scr_l4, scr_x3,
                             o_fin, None)
        sc.op("gpsimd", lambda h: h.engine_nop(),
              waits=(("gd", sc.n("gd")), ("yd", sc.n("yd"))), inc=None)
        sc.emit(st)
    return nc


SELH_CONST = np.zeros((2, 128), dtype=np.float32)
SELH_CONST[0, 0:64] = 1.0
SELH_CONST[1, 64:128] = 1.0


def _prep_L4_inputs(r3, core):
    g = (core // 4) * 4
    return {
        "q_in": np.asarray(r3[core]["o_q"]),
        "k_in": np.asarray(r3[core]["o_k"]),
        "r1all": np.stack([np.asarray(r3[g + j]["o_r1"]) for j in range(4)]),
        "selh_in": SELH_CONST,
    }


def _prep_L5_inputs(r3, r4, core):
    g = (core // 4) * 4
    return {
        "q_in": np.asarray(r3[core]["o_q"]),
        "k_in": np.asarray(r3[core]["o_k"]),
        "v_in": np.asarray(r3[core]["o_v"]),
        "r2all": np.stack([np.asarray(r4[g + j]["o_r2"]) for j in range(4)]),
        "ident": np.eye(128, dtype=np.float32),
    }


def _prep_L6_inputs(r3, r4, r5, inputs, core):
    g = (core // 4) * 4

    def col16(v):
        return np.ascontiguousarray(
            np.asarray(v, np.float32).reshape(-1, 128).T)

    return {
        "q_in": np.asarray(r3[core]["o_q"]),
        "si_in": np.asarray(r4[core]["o_si"]),
        "cs_in": np.asarray(r5[core]["o_cs"]),
        "zall": np.stack([np.asarray(r5[g + j]["o_z"]) for j in range(4)]),
        "kvall": np.stack([np.asarray(r5[g + j]["o_kvp"]) for j in range(4)]),
        "x2_in": np.asarray(r3[core]["o_x2"]),
        "woT": np.ascontiguousarray(
            np.asarray(inputs["L0_attn_b_Wo"], np.float32).T),
        "bo": col16(inputs["L0_attn_b_bo"]),
        "w1": np.asarray(inputs["L0_ffn_w1"], np.float32)[core],
        "b1": col16(np.asarray(inputs["L0_ffn_b1"], np.float32)[core]),
        "w2": np.asarray(inputs["L0_ffn_w2"], np.float32)[core],
        "b2": col16(np.asarray(inputs["L0_ffn_b2"], np.float32)[core]),
        "selh_in": SELH_CONST,
    }


def kernel(**inputs):
    cores = list(range(8))
    r1 = run_bass_kernel_spmd(
        build_L1(), [_prep_L1_inputs(inputs, c) for c in cores],
        core_ids=cores).results
    r2 = run_bass_kernel_spmd(
        build_L2(), [_prep_L2_inputs(r1, inputs, c) for c in cores],
        core_ids=cores).results
    r3 = run_bass_kernel_spmd(
        build_L3(), [_prep_L3_inputs(r1, r2, inputs, c) for c in cores],
        core_ids=cores).results
    r4 = run_bass_kernel_spmd(
        build_L4(), [_prep_L4_inputs(r3, c) for c in cores],
        core_ids=cores).results
    r5 = run_bass_kernel_spmd(
        build_L5(), [_prep_L5_inputs(r3, r4, c) for c in cores],
        core_ids=cores).results
    r6 = run_bass_kernel_spmd(
        build_L6(), [_prep_L6_inputs(r3, r4, r5, inputs, c) for c in cores],
        core_ids=cores).results
    out = np.concatenate(
        [np.asarray(r6[c]["o_fin"]).T for c in cores], axis=0)
    return np.ascontiguousarray(out.reshape(B, L, D), dtype=np.float32)

